# version 01
# baseline (speedup 1.0000x reference)
import contextlib
import math
import numpy as np
import ml_dtypes
import concourse.bass as bass
import concourse.mybir as mybir
from concourse.bass_utils import run_bass_kernel_spmd

F32 = mybir.dt.float32
BF16 = mybir.dt.bfloat16
AF = mybir.ActivationFunctionType
ALU = mybir.AluOpType
AX = mybir.AxisListType

B, T, E = 2, 4096, 4096
TOK = 1024
DCH = 32
S128 = 1.0 / math.sqrt(128.0)
EXPB = 30.0
LN_EPS = 1e-5

KT_QB = [
    [kt for gb in (0, 2, 4, 6) for kt in range(4 * gb, 4 * gb + 4)],
    list(range(32)),
]



def _rope_cos_sin():
    DH = 128
    inv = 1.0 / (500000.0 ** (np.arange(0, DH, 2, dtype=np.float32) / DH))
    inv = inv.astype(np.float32)
    wavelen = (2.0 * math.pi / inv).astype(np.float32)
    cut_lo = np.float32(8192 / 4.0)
    cut_hi = np.float32(8192 / 1.0)
    inv2 = np.where(wavelen > cut_hi, inv / np.float32(32.0), inv)
    smooth = ((np.float32(8192) / wavelen - np.float32(1.0)) / np.float32(3.0)).astype(np.float32)
    smoothed = ((1 - smooth) * inv2 / np.float32(32.0) + smooth * inv2).astype(np.float32)
    is_medium = ~(wavelen < cut_hi) & ~(wavelen > cut_lo)
    invf = np.where(is_medium, smoothed, inv2).astype(np.float32)
    pos = np.arange(T, dtype=np.float32)
    freqs = pos[:, None] * invf[None, :]
    emb = np.concatenate([freqs, freqs], axis=-1)
    return np.cos(emb).astype(np.float32), np.sin(emb).astype(np.float32)


def _clifford_rev_biv():
    g = [1.0, 1.0, 1.0]
    dim, n = 3, 8
    masks = sorted(range(n), key=lambda m: (bin(m).count('1'), m))
    invm = {m: i for i, m in enumerate(masks)}
    C = np.zeros((n, n, n), dtype=np.float32)
    for i, a in enumerate(masks):
        for j, bm in enumerate(masks):
            sw, at = 0, a >> 1
            while at:
                sw += bin(at & bm).count('1')
                at >>= 1
            sgn = -1.0 if sw & 1 else 1.0
            for t in range(dim):
                if a >> t & 1 and bm >> t & 1:
                    sgn *= g[t]
            C[i, j, invm[a ^ bm]] = sgn
    grades = np.array([bin(m).count('1') for m in masks])
    rev = np.where(grades * (grades - 1) // 2 % 2 == 1, -1.0, 1.0).astype(np.float32)
    biv = C[:, :, [4, 5, 6]]
    return rev, biv


def _own_tokens(g):
    return np.r_[512 * g:512 * (g + 1), 512 * (7 - g):512 * (8 - g)]


def _wimg(Wt, csize):
    nq = Wt.shape[1] // csize
    return np.ascontiguousarray(
        Wt.reshape(DCH, 128, nq, csize).transpose(2, 1, 0, 3).reshape(nq, 128, DCH * csize))


def _img_cols(Wt):
    Fd = Wt.shape[1]
    return np.ascontiguousarray(Wt.reshape(DCH, 128, Fd).transpose(1, 0, 2).reshape(128, DCH * Fd))


class Sem:
    def __init__(self, stk, nc, name):
        self.h = stk.enter_context(nc.semaphore(name))
        self.n = 0

    def inc(self, inst, k=1):
        inst.then_inc(self.h, k)
        self.n += k
        return self.n


def _mk_sems(stk, nc, names):
    return {k: Sem(stk, nc, k) for k in names}


def _barrier(nc, sem):
    for e in (nc.tensor, nc.vector, nc.scalar, nc.gpsimd, nc.sync):
        sem.inc(e.nop(), 1)
    for e in (nc.tensor, nc.vector, nc.scalar, nc.gpsimd, nc.sync):
        e.wait_ge(sem.h, sem.n)



def emit_ln(nc, s, P, xdram, xn_s, bufs):
    xb, xsq = bufs["xb"], bufs["xsq"]
    mn_s, t1_s, msq_s, rt_s, rstd_s = bufs["rows"]
    nm_b, rs_b = bufs["nm_b"], bufs["rs_b"]
    ones_f, ones1_f = bufs["ones_f"], bufs["ones1_f"]
    mm0, ac0, xl0 = s["mm"].n, s["ac"].n, s["xl"].n
    for dc in range(DCH):
        b = xb[dc % 2]
        if dc >= 2:
            nc.sync.wait_ge(s["mm"].h, mm0 + 2 * (dc - 1))
        s["xl"].inc(nc.sync.dma_start(out=b[:, :], in_=xdram[128 * dc:128 * (dc + 1), :]), 16)
        nc.scalar.wait_ge(s["xl"].h, s["xl"].n)
        if dc >= 1:
            nc.scalar.wait_ge(s["mm"].h, mm0 + 2 * dc)
        s["ac"].inc(nc.scalar.activation(xsq[:, :], b[:, :], AF.Square), 1)
        nc.tensor.wait_ge(s["xl"].h, s["xl"].n)
        nc.tensor.matmul(P[0][0:1, :], ones_f[:, :], b[:, 0:512],
                         start=(dc == 0), stop=(dc == DCH - 1))
        s["mm"].inc(nc.tensor.matmul(P[1][0:1, :], ones_f[:, :], b[:, 512:1024],
                                     start=(dc == 0), stop=(dc == DCH - 1)), 1)
        nc.tensor.wait_ge(s["ac"].h, s["ac"].n)
        nc.tensor.matmul(P[2][0:1, :], ones_f[:, :], xsq[:, 0:512],
                         start=(dc == 0), stop=(dc == DCH - 1))
        s["mm"].inc(nc.tensor.matmul(P[3][0:1, :], ones_f[:, :], xsq[:, 512:1024],
                                     start=(dc == 0), stop=(dc == DCH - 1)), 1)
    def dv(fn):
        v = s["dv"].inc(fn(), 1)
        nc.vector.wait_ge(s["dv"].h, v)
    nc.vector.wait_ge(s["mm"].h, s["mm"].n)
    dv(lambda: nc.vector.tensor_scalar(mn_s[0:1, 0:512], P[0][0:1, :], -1.0 / 4096.0, None, ALU.mult))
    dv(lambda: nc.vector.tensor_scalar(mn_s[0:1, 512:1024], P[1][0:1, :], -1.0 / 4096.0, None, ALU.mult))
    dv(lambda: nc.vector.tensor_scalar(t1_s[0:1, 0:512], P[2][0:1, :], 1.0 / 4096.0, None, ALU.mult))
    dv(lambda: nc.vector.tensor_scalar(t1_s[0:1, 512:1024], P[3][0:1, :], 1.0 / 4096.0, None, ALU.mult))
    dv(lambda: nc.vector.tensor_mul(msq_s[0:1, :], mn_s[0:1, :], mn_s[0:1, :]))
    dv(lambda: nc.vector.scalar_tensor_tensor(t1_s[0:1, :], t1_s[0:1, :], LN_EPS, msq_s[0:1, :],
                                              op0=ALU.add, op1=ALU.subtract))
    dv(lambda: nc.vector.reciprocal(rt_s[0:1, :], t1_s[0:1, :]))
    nc.scalar.wait_ge(s["dv"].h, s["dv"].n)
    s["ac"].inc(nc.scalar.activation(rstd_s[0:1, :], rt_s[0:1, :], AF.Sqrt), 1)
    nc.tensor.wait_ge(s["ac"].h, s["ac"].n)
    nc.tensor.wait_ge(s["dv"].h, s["dv"].n)
    s["mm"].inc(nc.tensor.matmul(P[0][:, :], ones1_f[:, :], mn_s[0:1, 0:512], start=True, stop=True), 1)
    s["mm"].inc(nc.tensor.matmul(P[1][:, :], ones1_f[:, :], mn_s[0:1, 512:1024], start=True, stop=True), 1)
    s["mm"].inc(nc.tensor.matmul(P[2][:, :], ones1_f[:, :], rstd_s[0:1, 0:512], start=True, stop=True), 1)
    s["mm"].inc(nc.tensor.matmul(P[3][:, :], ones1_f[:, :], rstd_s[0:1, 512:1024], start=True, stop=True), 1)
    nc.scalar.wait_ge(s["mm"].h, s["mm"].n)
    s["ac"].inc(nc.scalar.copy(nm_b[:, 0:512], P[0][:, :]), 1)
    s["ac"].inc(nc.scalar.copy(nm_b[:, 512:1024], P[1][:, :]), 1)
    s["ac"].inc(nc.scalar.copy(rs_b[:, 0:512], P[2][:, :]), 1)
    ac_fin = s["ac"].inc(nc.scalar.copy(rs_b[:, 512:1024], P[3][:, :]), 1)
    dvb = {}
    for dc in range(DCH):
        b = xb[dc % 2]
        if dc == 0:
            nc.sync.wait_ge(s["ac"].h, ac_fin)
        if dc >= 2:
            nc.sync.wait_ge(s["dv"].h, dvb[dc - 2])
        s["xl"].inc(nc.sync.dma_start(out=b[:, :], in_=xdram[128 * dc:128 * (dc + 1), :]), 16)
        nc.vector.wait_ge(s["xl"].h, s["xl"].n)
        if dc == 0:
            nc.vector.wait_ge(s["ac"].h, ac_fin)
        else:
            nc.vector.wait_ge(s["dv"].h, dvb[dc - 1])
        v1 = s["dv"].inc(nc.vector.tensor_add(xsq[:, :], b[:, :], nm_b[:, :]), 1)
        nc.vector.wait_ge(s["dv"].h, v1)
        dvb[dc] = s["dv"].inc(nc.vector.tensor_mul(xn_s[:, TOK * dc:TOK * (dc + 1)], xsq[:, :], rs_b[:, :]), 1)



def build_l0a():
    nc = bass.Bass("TRN2", target_bir_lowering=False)
    xT = nc.dram_tensor("xT", [4096, TOK], F32, kind="ExternalInput").ap()
    wq = nc.dram_tensor("wq", [32, 128, 4096], F32, kind="ExternalInput").ap()
    wkq = nc.dram_tensor("wkq", [4, 128, 8192], F32, kind="ExternalInput").ap()
    wvq = nc.dram_tensor("wvq", [4, 128, 8192], F32, kind="ExternalInput").ap()
    wb = nc.dram_tensor("wb", [32, 128, 4096], F32, kind="ExternalInput").ap()
    wg = nc.dram_tensor("wg", [128, 256], F32, kind="ExternalInput").ap()
    a0 = nc.dram_tensor("a0", [128, 4096], F32, kind="ExternalInput").ap()
    b0 = nc.dram_tensor("b0", [128, 4096], F32, kind="ExternalInput").ap()
    bb = nc.dram_tensor("bb", [128, 32], F32, kind="ExternalInput").ap()
    cosq = nc.dram_tensor("cosq", [128, TOK], BF16, kind="ExternalInput").ap()
    sinq = nc.dram_tensor("sinq", [128, TOK], F32, kind="ExternalInput").ap()
    cosk = nc.dram_tensor("cosk", [128, TOK], F32, kind="ExternalInput").ap()
    sink = nc.dram_tensor("sink", [128, TOK], F32, kind="ExternalInput").ap()
    prot = nc.dram_tensor("prot", [128, 128], BF16, kind="ExternalInput").ap()
    exp0 = nc.dram_tensor("exp0", [8, 128], BF16, kind="ExternalInput").ap()
    qd = nc.dram_tensor("qd", [4096, TOK], BF16, kind="ExternalOutput").ap()
    kvd = nc.dram_tensor("kvd", [TOK, 2048], BF16, kind="ExternalOutput").ap()
    gd = nc.dram_tensor("gd", [4096, TOK], BF16, kind="ExternalOutput").ap()

    with contextlib.ExitStack() as stk:
        sb = lambda name, shape, dt: stk.enter_context(nc.sbuf_tensor(name, shape, dt))
        xn_s = sb("xn_s", [128, DCH * TOK], BF16)
        xb = [sb("xb0", [128, TOK], F32), sb("xb1", [128, TOK], F32)]
        xsq = sb("xsq", [128, TOK], F32)
        nm_b = sb("nm_b", [128, TOK], F32)
        rs_b = sb("rs_b", [128, TOK], F32)
        ones_f = sb("ones_f", [128, 1], F32)
        ones1_f = sb("ones1_f", [1, 128], F32)
        w2 = [sb("w20", [128, 4096], BF16), sb("w21", [128, 4096], BF16)]
        kw = [sb("kw0", [128, 8192], BF16), sb("kw1", [128, 8192], BF16)]
        qtmp = [sb(f"qtmp{i}", [128, 512], BF16) for i in range(2)]
        t1b = [sb(f"t1b{i}", [128, 512], BF16) for i in range(2)]
        t2b = [sb(f"t2b{i}", [128, 512], BF16) for i in range(2)]
        qo = [sb(f"qo{i}", [128, 512], BF16) for i in range(2)]
        cosq_s = sb("cosq_s", [128, TOK], BF16)
        sinq_s = sb("sinq_s", [128, TOK], F32)
        cosk_s = sb("cosk_s", [128, TOK], F32)
        sink_s = sb("sink_s", [128, TOK], F32)
        prot_s = sb("prot_s", [128, 128], BF16)
        ktm = [sb(f"ktm{i}", [128, 256], F32) for i in range(2)]
        ksn = [sb(f"ksn{i}", [128, 256], F32) for i in range(2)]
        kvo = [sb(f"kvo{i}", [128, 256], BF16) for i in range(2)]
        wg_s = sb("wg_s", [128, 256], BF16)
        a0_s = sb("a0_s", [128, 4096], BF16)
        b0_s = sb("b0_s", [128, 4096], BF16)
        bb_s = sb("bb_s", [128, 32], F32)
        exp0_s = sb("exp0_s", [8, 128], BF16)
        lg_s = sb("lg_s", [128, 8], F32)
        ltmp = sb("ltmp", [128, 8], F32)
        m1_s = sb("m1_s", [128, 1], F32)
        m2_s = sb("m2_s", [128, 1], F32)
        den_s = sb("den_s", [128, 1], F32)
        w1c = sb("w1c", [128, 1], F32)
        w2c = sb("w2c", [128, 1], F32)
        msk1 = sb("msk1", [128, 8], F32)
        msk2 = sb("msk2", [128, 8], F32)
        rte = sb("rte", [128, 8], BF16)
        rT_s = sb("rT_s", [8, TOK], BF16)
        re_s = sb("re_s", [128, TOK], BF16)
        hw_s = sb("hw_s", [128, TOK], BF16)
        go = [sb(f"go{i}", [128, 512], BF16) for i in range(2)]
        ident = sb("ident", [128, 128], BF16)
        P = [stk.enter_context(nc.psum_tensor(f"P{i}", [128, 512], F32)) for i in range(7)]
        PT8 = stk.enter_context(nc.psum_tensor("PT8", [8, 128], BF16))
        s = _mk_sems(stk, nc, ["ld", "xl", "w", "mm", "ac", "dv", "st", "ph"])

        s["ld"].inc(nc.sync.dma_start(out=cosq_s[:, :], in_=cosq[:, :]), 16)
        s["ld"].inc(nc.sync.dma_start(out=sinq_s[:, :], in_=sinq[:, :]), 16)
        s["ld"].inc(nc.sync.dma_start(out=cosk_s[:, :], in_=cosk[:, :]), 16)
        s["ld"].inc(nc.sync.dma_start(out=sink_s[:, :], in_=sink[:, :]), 16)
        s["ld"].inc(nc.sync.dma_start(out=prot_s[:, :], in_=prot[:, :]), 16)
        s["ld"].inc(nc.sync.dma_start(out=exp0_s[:, :], in_=exp0[:, :]), 16)
        s["ld"].inc(nc.sync.dma_start(out=bb_s[:, :], in_=bb[:, :]), 16)
        s["ld"].inc(nc.gpsimd.dma_start(out=wg_s[:, :], in_=wg[:, :]), 16)
        s["ld"].inc(nc.gpsimd.dma_start(out=a0_s[:, :], in_=a0[:, :]), 16)
        s["ld"].inc(nc.gpsimd.dma_start(out=b0_s[:, :], in_=b0[:, :]), 16)
        nc.vector.memset(ones_f[:, :], 1.0)
        s["dv"].inc(nc.vector.memset(ones1_f[:, :], 1.0), 1)
        nc.gpsimd.memset(ident[:, :], 0.0)
        s["dv"].inc(nc.gpsimd.affine_select(
            out=ident[:, :], in_=ident[:, :], compare_op=ALU.not_equal,
            fill=1.0, base=0, pattern=[[-1, 128]], channel_multiplier=1), 1)
        nc.tensor.wait_ge(s["dv"].h, s["dv"].n)
        nc.tensor.wait_ge(s["ld"].h, s["ld"].n)

        rows = [nm_b[0:1, :], xsq[0:1, :], xb[0][0:1, :], xb[1][0:1, :], rs_b[0:1, :]]
        emit_ln(nc, s, P, xT, xn_s, dict(
            xb=xb, xsq=xsq, rows=rows, nm_b=nm_b, rs_b=rs_b, ones_f=ones_f, ones1_f=ones1_f))
        _barrier(nc, s["ph"])

        mm_q = {}
        ac_q = {}
        dv_q = {}
        st_q = {}
        for oc in range(32):
            wbuf = w2[oc % 2]
            if oc >= 2:
                nc.gpsimd.wait_ge(s["mm"].h, mm_q[(oc - 2, 1)])
            s["w"].inc(nc.gpsimd.dma_start(out=wbuf[:, :], in_=wq[oc, :, :]), 16)
            nc.tensor.wait_ge(s["w"].h, s["w"].n)
            for tb in range(2):
                psq = P[(oc % 2) * 2 + tb]
                psr = P[4 + tb]
                if oc >= 2:
                    nc.tensor.wait_ge(s["ac"].h, ac_q[(oc - 2, tb)])
                for dc in range(DCH):
                    inst = nc.tensor.matmul(
                        psq[:, :], wbuf[:, 128 * dc:128 * (dc + 1)],
                        xn_s[:, TOK * dc + 512 * tb:TOK * dc + 512 * (tb + 1)],
                        start=(dc == 0), stop=(dc == DCH - 1))
                mm_q[(oc, tb)] = s["mm"].inc(inst, 1)
                nc.scalar.wait_ge(s["mm"].h, mm_q[(oc, tb)])
                if oc >= 1:
                    nc.scalar.wait_ge(s["dv"].h, dv_q[(oc - 1, tb)])
                ac_q[(oc, tb)] = s["ac"].inc(nc.scalar.copy(qtmp[tb][:, :], psq[:, :]), 1)
                nc.tensor.wait_ge(s["ac"].h, ac_q[(oc, tb)])
                if oc >= 1:
                    nc.tensor.wait_ge(s["dv"].h, dv_q[(oc - 1, tb)] + 1)
                mmr = s["mm"].inc(nc.tensor.matmul(psr[:, :], prot_s[:, :], qtmp[tb][:, :],
                                                   start=True, stop=True), 1)
                nc.vector.wait_ge(s["mm"].h, mmr)
                if oc >= 1:
                    nc.vector.wait_ge(s["st"].h, st_q[(oc - 1, tb)])
                dv_q[(oc, tb)] = s["dv"].inc(nc.vector.tensor_mul(
                    t1b[tb][:, :], qtmp[tb][:, :], cosq_s[:, 512 * tb:512 * (tb + 1)]), 1)
                s["dv"].inc(nc.vector.tensor_mul(
                    t2b[tb][:, :], psr[:, :], sinq_s[:, 512 * tb:512 * (tb + 1)]), 1)
                nc.vector.wait_ge(s["dv"].h, s["dv"].n)
                dvo = s["dv"].inc(nc.vector.tensor_add(qo[tb][:, :], t1b[tb][:, :], t2b[tb][:, :]), 1)
                nc.sync.wait_ge(s["dv"].h, dvo)
                st_q[(oc, tb)] = s["st"].inc(nc.sync.dma_start(
                    out=qd[128 * oc:128 * (oc + 1), 512 * tb:512 * (tb + 1)], in_=qo[tb][:, :]), 16)
        _barrier(nc, s["ph"])

        mm_kv = {}
        fin_kv = {}
        st_kv = {}
        for part in range(8):
            isv = part >= 4
            qq = part % 4
            wsl = kw[part % 2]
            if part >= 2:
                nc.gpsimd.wait_ge(s["mm"].h, mm_kv[(part - 2, 7)])
            s["w"].inc(nc.gpsimd.dma_start(out=wsl[:, :], in_=(wvq if isv else wkq)[qq, :, :]), 16)
            nc.tensor.wait_ge(s["w"].h, s["w"].n)
            for tc in range(8):
                pb = P[tc % 2]
                idx = part * 8 + tc
                if idx >= 2:
                    pk, pt = divmod(idx - 2, 8)
                    kind, val = fin_kv[(pk, pt)]
                    nc.tensor.wait_ge(s[kind].h, val)
                for dc in range(DCH):
                    inst = nc.tensor.matmul(
                        pb[:, 0:256],
                        xn_s[:, TOK * dc + 128 * tc:TOK * dc + 128 * (tc + 1)],
                        wsl[:, 256 * dc:256 * (dc + 1)],
                        start=(dc == 0), stop=(dc == DCH - 1))
                mm_kv[(part, tc)] = s["mm"].inc(inst, 1)
                kb = kvo[tc % 2]
                if idx >= 2:
                    nc.vector.wait_ge(s["st"].h, st_kv[(idx - 2)])
                    nc.scalar.wait_ge(s["st"].h, st_kv[(idx - 2)])
                if not isv:
                    nc.vector.wait_ge(s["mm"].h, mm_kv[(part, tc)])
                    kt_, ks_ = ktm[tc % 2], ksn[tc % 2]
                    for gch in range(2):
                        c0 = 128 * gch
                        s["dv"].inc(nc.vector.tensor_mul(
                            kt_[:, c0:c0 + 128], pb[:, c0:c0 + 128],
                            cosk_s[:, 128 * tc:128 * (tc + 1)]), 1)
                        s["dv"].inc(nc.vector.tensor_mul(
                            ks_[:, c0:c0 + 64], pb[:, c0 + 64:c0 + 128],
                            sink_s[:, 128 * tc:128 * tc + 64]), 1)
                        s["dv"].inc(nc.vector.tensor_mul(
                            ks_[:, c0 + 64:c0 + 128], pb[:, c0:c0 + 64],
                            sink_s[:, 128 * tc + 64:128 * (tc + 1)]), 1)
                    nc.vector.wait_ge(s["dv"].h, s["dv"].n)
                    v = s["dv"].inc(nc.vector.tensor_add(kb[:, :], kt_[:, :], ks_[:, :]), 1)
                    fin_kv[(part, tc)] = ("dv", v)
                    nc.sync.wait_ge(s["dv"].h, v)
                else:
                    nc.scalar.wait_ge(s["mm"].h, mm_kv[(part, tc)])
                    v = s["ac"].inc(nc.scalar.copy(kb[:, :], pb[:, 0:256]), 1)
                    fin_kv[(part, tc)] = ("ac", v)
                    nc.sync.wait_ge(s["ac"].h, v)
                col0 = 1024 * (1 if isv else 0) + 256 * qq
                st_kv[idx] = s["st"].inc(nc.sync.dma_start(
                    out=kvd[128 * tc:128 * (tc + 1), col0:col0 + 256], in_=kb[:, :]), 16)
        _barrier(nc, s["ph"])

        def dv(fn):
            v = s["dv"].inc(fn(), 1)
            nc.vector.wait_ge(s["dv"].h, v)
            return v
        for tc in range(8):
            for dc in range(DCH):
                inst = nc.tensor.matmul(
                    P[6][:, 0:8],
                    xn_s[:, TOK * dc + 128 * tc:TOK * dc + 128 * (tc + 1)],
                    wg_s[:, 8 * dc:8 * (dc + 1)],
                    start=(dc == 0), stop=(dc == DCH - 1))
            mmv = s["mm"].inc(inst, 1)
            nc.vector.wait_ge(s["mm"].h, mmv)
            dv(lambda: nc.vector.tensor_copy(lg_s[:, :], P[6][:, 0:8]))
            dv(lambda: nc.vector.reduce_max(m1_s[:, :], lg_s[:, :], axis=AX.X))
            dv(lambda: nc.vector.tensor_scalar(ltmp[:, :], lg_s[:, :], m1_s[:, 0:1], -1e30,
                                               op0=ALU.is_equal, op1=ALU.mult))
            dv(lambda: nc.vector.tensor_add(ltmp[:, :], ltmp[:, :], lg_s[:, :]))
            dv(lambda: nc.vector.reduce_max(m2_s[:, :], ltmp[:, :], axis=AX.X))
            dv(lambda: nc.vector.tensor_sub(den_s[:, :], m2_s[:, :], m1_s[:, :]))
            nc.scalar.wait_ge(s["dv"].h, s["dv"].n)
            acv = s["ac"].inc(nc.scalar.activation(den_s[:, :], den_s[:, :], AF.Exp), 1)
            nc.vector.wait_ge(s["ac"].h, acv)
            dv(lambda: nc.vector.tensor_scalar(den_s[:, :], den_s[:, :], 1.0, None, ALU.add))
            dv(lambda: nc.vector.reciprocal(w1c[:, :], den_s[:, :]))
            dv(lambda: nc.vector.tensor_scalar(w2c[:, :], w1c[:, :], -1.0, 1.0,
                                               op0=ALU.mult, op1=ALU.add))
            dv(lambda: nc.vector.tensor_scalar(msk1[:, :], lg_s[:, :], m1_s[:, 0:1], None,
                                               op0=ALU.is_equal))
            dv(lambda: nc.vector.tensor_scalar(msk2[:, :], ltmp[:, :], m2_s[:, 0:1], None,
                                               op0=ALU.is_equal))
            dv(lambda: nc.vector.tensor_scalar(msk1[:, :], msk1[:, :], w1c[:, 0:1], None, op0=ALU.mult))
            dv(lambda: nc.vector.tensor_scalar(msk2[:, :], msk2[:, :], w2c[:, 0:1], None, op0=ALU.mult))
            dvv = dv(lambda: nc.vector.tensor_add(rte[:, :], msk1[:, :], msk2[:, :]))
            nc.tensor.wait_ge(s["dv"].h, dvv)
            if tc >= 1:
                nc.tensor.wait_ge(s["ac"].h, s["ac"].n)
            mmt = s["mm"].inc(nc.tensor.transpose(PT8[0:8, :], rte[:, :], ident[:, :]), 1)
            nc.scalar.wait_ge(s["mm"].h, mmt)
            s["ac"].inc(nc.scalar.copy(rT_s[0:8, 128 * tc:128 * (tc + 1)], PT8[0:8, :]), 1)
        nc.tensor.wait_ge(s["ac"].h, s["ac"].n)
        for tb in range(2):
            mmv = s["mm"].inc(nc.tensor.matmul(P[4 + tb][:, :], exp0_s[:, :],
                                               rT_s[0:8, 512 * tb:512 * (tb + 1)],
                                               start=True, stop=True), 1)
        nc.scalar.wait_ge(s["mm"].h, s["mm"].n)
        s["ac"].inc(nc.scalar.copy(re_s[:, 0:512], P[4][:, :]), 1)
        acv = s["ac"].inc(nc.scalar.copy(re_s[:, 512:1024], P[5][:, :]), 1)
        for tb in range(2):
            for dc in range(DCH):
                inst = nc.tensor.matmul(
                    P[2 + tb][:, :], a0_s[:, 128 * dc:128 * (dc + 1)],
                    xn_s[:, TOK * dc + 512 * tb:TOK * dc + 512 * (tb + 1)],
                    start=(dc == 0), stop=(dc == DCH - 1))
            mmv = s["mm"].inc(inst, 1)
            nc.vector.wait_ge(s["mm"].h, mmv)
            nc.vector.wait_ge(s["ac"].h, acv)
            s["dv"].inc(nc.vector.tensor_mul(hw_s[:, 512 * tb:512 * (tb + 1)], P[2 + tb][:, :],
                                             re_s[:, 512 * tb:512 * (tb + 1)]), 1)
        dv_hw = s["dv"].n
        mm_g = {}
        ac_g = {}
        st_g = {}
        for oc in range(32):
            wbuf = w2[oc % 2]
            if oc >= 2:
                nc.gpsimd.wait_ge(s["mm"].h, mm_g[(oc - 2, 1)])
            s["w"].inc(nc.gpsimd.dma_start(out=wbuf[:, :], in_=wb[oc, :, :]), 16)
            nc.tensor.wait_ge(s["w"].h, s["w"].n)
            if oc == 0:
                nc.tensor.wait_ge(s["dv"].h, dv_hw)
            for tb in range(2):
                pb = P[(oc % 2) * 2 + tb]
                if oc >= 2:
                    nc.tensor.wait_ge(s["ac"].h, ac_g[(oc - 2, tb)])
                for dc in range(DCH):
                    nc.tensor.matmul(pb[:, :], wbuf[:, 128 * dc:128 * (dc + 1)],
                                     xn_s[:, TOK * dc + 512 * tb:TOK * dc + 512 * (tb + 1)],
                                     start=(dc == 0), stop=False)
                inst = nc.tensor.matmul(pb[:, :], b0_s[:, 128 * oc:128 * (oc + 1)],
                                        hw_s[:, 512 * tb:512 * (tb + 1)], start=False, stop=True)
                mm_g[(oc, tb)] = s["mm"].inc(inst, 1)
                nc.scalar.wait_ge(s["mm"].h, mm_g[(oc, tb)])
                if oc >= 1:
                    nc.scalar.wait_ge(s["st"].h, st_g[(oc - 1, tb)])
                ac_g[(oc, tb)] = s["ac"].inc(nc.scalar.activation(
                    go[tb][:, :], pb[:, :], AF.Silu, bias=bb_s[:, oc:oc + 1]), 1)
                nc.sync.wait_ge(s["ac"].h, ac_g[(oc, tb)])
                st_g[(oc, tb)] = s["st"].inc(nc.sync.dma_start(
                    out=gd[128 * oc:128 * (oc + 1), 512 * tb:512 * (tb + 1)], in_=go[tb][:, :]), 16)
        nc.sync.wait_ge(s["st"].h, s["st"].n)
    return nc



def build_l0b():
    nc = bass.Bass("TRN2", target_bir_lowering=False)
    xT = nc.dram_tensor("xT", [4096, TOK], F32, kind="ExternalInput").ap()
    qd = nc.dram_tensor("qd", [4096, TOK], BF16, kind="ExternalInput").ap()
    gd = nc.dram_tensor("gd", [4096, TOK], BF16, kind="ExternalInput").ap()
    kvg = nc.dram_tensor("kvg", [4096, 2048], BF16, kind="ExternalInput").ap()
    wo = nc.dram_tensor("wo", [32, 128, 4096], F32, kind="ExternalInput").ap()
    mask = nc.dram_tensor("mask", [2, 32, 128, 512], BF16, kind="ExternalInput").ap()
    x1d = nc.dram_tensor("x1d", [4096, TOK], F32, kind="ExternalOutput").ap()

    with contextlib.ExitStack() as stk:
        sb = lambda name, shape, dt: stk.enter_context(nc.sbuf_tensor(name, shape, dt))
        mask_s = sb("mask_s", [128, 2 * 32 * 512], BF16)
        ctx_s = sb("ctx_s", [128, DCH * TOK], BF16)
        kfm = sb("kfm", [128, 4096], BF16)
        vkm = sb("vkm", [128, 4096], BF16)
        q_s = sb("q_s", [128, TOK], BF16)
        ebuf = [sb(f"ebuf{i}", [128, 512], BF16) for i in range(3)]
        rb_s = sb("rb_s", [128, 512], F32)
        rrow = sb("rrow", [1, 512], F32)
        biasB = sb("biasB", [128, 1], F32)
        ones_b = sb("ones_b", [128, 1], BF16)
        ones1_f = sb("ones1_f", [1, 128], F32)
        w2 = [sb("w20", [128, 4096], BF16), sb("w21", [128, 4096], BF16)]
        g_s = [sb(f"g_s{i}", [128, TOK], F32) for i in range(2)]
        x_s = [sb(f"x_s{i}", [128, TOK], F32) for i in range(2)]
        t_s = [sb(f"t_s{i}", [128, 512], F32) for i in range(2)]
        x1o = [sb(f"x1o{i}", [128, 512], F32) for i in range(2)]
        P = [stk.enter_context(nc.psum_tensor(f"P{i}", [128, 512], F32)) for i in range(7)]
        s = _mk_sems(stk, nc, ["ld", "kf", "w", "mm", "ac", "dv", "st", "ph"])

        s["ld"].inc(nc.sync.dma_start(
            out=mask_s[:, :].rearrange("p (a b q) -> p a b q", a=2, b=32),
            in_=mask.rearrange("a b p q -> p a b q")), 16)
        nc.vector.memset(biasB[:, :], -EXPB)
        nc.vector.memset(ones_b[:, :], 1.0)
        s["dv"].inc(nc.vector.memset(ones1_f[:, :], 1.0), 1)
        nc.tensor.wait_ge(s["dv"].h, s["dv"].n)
        nc.scalar.wait_ge(s["ld"].h, s["ld"].n)
        nc.vector.wait_ge(s["ld"].h, s["ld"].n)

        mm_headdone = {}
        ac_exp = {}
        dv_msk = {}
        mm_ctx = {}
        ac_rb = {}
        dv_evac = {}
        for h in range(32):
            kh = h // 4
            if h >= 1:
                nc.sync.wait_ge(s["mm"].h, mm_headdone[h - 1])
            if h % 4 == 0:
                if h >= 1:
                    nc.gpsimd.wait_ge(s["mm"].h, mm_headdone[h - 1])
                s["kf"].inc(nc.sync.dma_start(out=kfm[:, :], in_=kvg[:, 128 * kh:128 * (kh + 1)],
                                              transpose=True), 16)
                s["kf"].inc(nc.gpsimd.dma_start(
                    out=vkm[:, :].rearrange("p (a f) -> p a f", a=32),
                    in_=kvg[:, 1024 + 128 * kh:1024 + 128 * (kh + 1)].rearrange(
                        "(a p) f -> p a f", p=128)), 16)
            s["kf"].inc(nc.sync.dma_start(out=q_s[:, :], in_=qd[128 * h:128 * (h + 1), :]), 16)
            nc.tensor.wait_ge(s["kf"].h, s["kf"].n)
            for qb in range(2):
                kts = KT_QB[qb]
                nkt = len(kts)
                for i, kt in enumerate(kts):
                    if i >= 2:
                        nc.tensor.wait_ge(s["ac"].h, ac_exp[(h, qb, i - 2)])
                    mmv = s["mm"].inc(nc.tensor.matmul(
                        P[i % 2][:, :], kfm[:, 128 * kt:128 * (kt + 1)],
                        q_s[:, 512 * qb:512 * (qb + 1)], start=True, stop=True), 1)
                    nc.scalar.wait_ge(s["mm"].h, mmv)
                    if i >= 3:
                        nc.scalar.wait_ge(s["mm"].h, mm_ctx[(h, qb, i - 3)])
                    ac_exp[(h, qb, i)] = s["ac"].inc(nc.scalar.activation(
                        ebuf[i % 3][:, :], P[i % 2][:, :], AF.Exp, bias=biasB[:, 0:1]), 1)
                    nc.vector.wait_ge(s["ac"].h, ac_exp[(h, qb, i)])
                    dv_msk[(h, qb, i)] = s["dv"].inc(nc.vector.tensor_mul(
                        ebuf[i % 3][:, :], ebuf[i % 3][:, :],
                        mask_s[:, 16384 * qb + 512 * kt:16384 * qb + 512 * (kt + 1)]), 1)
                    nc.tensor.wait_ge(s["dv"].h, dv_msk[(h, qb, i)])
                    if i == 0 and (h, qb) != (0, 0):
                        ph, pqb = (h, qb - 1) if qb == 1 else (h - 1, 1)
                        nc.tensor.wait_ge(s["dv"].h, dv_evac[(ph, pqb)])
                    nc.tensor.matmul(P[4][:, :], vkm[:, 128 * kt:128 * (kt + 1)],
                                     ebuf[i % 3][:, :], start=(i == 0), stop=(i == nkt - 1))
                    mm_ctx[(h, qb, i)] = s["mm"].inc(nc.tensor.matmul(
                        P[5][0:1, :], ones_b[:, :], ebuf[i % 3][:, :],
                        start=(i == 0), stop=(i == nkt - 1)), 1)
                nc.vector.wait_ge(s["mm"].h, mm_ctx[(h, qb, nkt - 1)])
                dvr = s["dv"].inc(nc.vector.reciprocal(rrow[0:1, :], P[5][0:1, :]), 1)
                nc.tensor.wait_ge(s["dv"].h, dvr)
                if (h, qb) != (0, 0):
                    ph, pqb = (h, qb - 1) if qb == 1 else (h - 1, 1)
                    nc.tensor.wait_ge(s["ac"].h, ac_rb[(ph, pqb)])
                mmb = s["mm"].inc(nc.tensor.matmul(P[6][:, :], ones1_f[:, :], rrow[0:1, :],
                                                   start=True, stop=True), 1)
                nc.scalar.wait_ge(s["mm"].h, mmb)
                if (h, qb) != (0, 0):
                    ph, pqb = (h, qb - 1) if qb == 1 else (h - 1, 1)
                    nc.scalar.wait_ge(s["dv"].h, dv_evac[(ph, pqb)])
                ac_rb[(h, qb)] = s["ac"].inc(nc.scalar.copy(rb_s[:, :], P[6][:, :]), 1)
                nc.vector.wait_ge(s["ac"].h, ac_rb[(h, qb)])
                dv_evac[(h, qb)] = s["dv"].inc(nc.vector.tensor_mul(
                    ctx_s[:, 1024 * h + 512 * qb:1024 * h + 512 * (qb + 1)],
                    P[4][:, :], rb_s[:, :]), 1)
            mm_headdone[h] = mm_ctx[(h, 1, 31)]
        _barrier(nc, s["ph"])

        mm_wo = {}
        dv_wo = {}
        st_wo = {}
        for oc in range(32):
            wbuf = w2[oc % 2]
            if oc >= 2:
                nc.gpsimd.wait_ge(s["mm"].h, mm_wo[(oc - 2, 1)])
                nc.gpsimd.wait_ge(s["dv"].h, dv_wo[(oc - 2, 1)])
                nc.sync.wait_ge(s["dv"].h, dv_wo[(oc - 2, 1)])
            s["w"].inc(nc.gpsimd.dma_start(out=wbuf[:, :], in_=wo[oc, :, :]), 16)
            s["w"].inc(nc.gpsimd.dma_start(out=g_s[oc % 2][:, :], in_=gd[128 * oc:128 * (oc + 1), :]), 16)
            s["w"].inc(nc.sync.dma_start(out=x_s[oc % 2][:, :], in_=xT[128 * oc:128 * (oc + 1), :]), 16)
            nc.tensor.wait_ge(s["w"].h, s["w"].n)
            for tb in range(2):
                pb = P[(oc % 2) * 2 + tb]
                if oc >= 2:
                    nc.tensor.wait_ge(s["dv"].h, dv_wo[(oc - 2, tb)])
                for dc in range(DCH):
                    inst = nc.tensor.matmul(
                        pb[:, :], wbuf[:, 128 * dc:128 * (dc + 1)],
                        ctx_s[:, TOK * dc + 512 * tb:TOK * dc + 512 * (tb + 1)],
                        start=(dc == 0), stop=(dc == DCH - 1))
                mm_wo[(oc, tb)] = s["mm"].inc(inst, 1)
                nc.vector.wait_ge(s["mm"].h, mm_wo[(oc, tb)])
                if oc >= 1:
                    nc.vector.wait_ge(s["st"].h, st_wo[(oc - 1, tb)])
                v1 = s["dv"].inc(nc.vector.tensor_mul(
                    t_s[tb][:, :], pb[:, :], g_s[oc % 2][:, 512 * tb:512 * (tb + 1)]), 1)
                nc.vector.wait_ge(s["dv"].h, v1)
                dv_wo[(oc, tb)] = s["dv"].inc(nc.vector.tensor_add(
                    x1o[tb][:, :], t_s[tb][:, :], x_s[oc % 2][:, 512 * tb:512 * (tb + 1)]), 1)
                nc.sync.wait_ge(s["dv"].h, dv_wo[(oc, tb)])
                st_wo[(oc, tb)] = s["st"].inc(nc.sync.dma_start(
                    out=x1d[128 * oc:128 * (oc + 1), 512 * tb:512 * (tb + 1)], in_=x1o[tb][:, :]), 16)
        nc.sync.wait_ge(s["st"].h, s["st"].n)
    return nc



def build_l1a():
    nc = bass.Bass("TRN2", target_bir_lowering=False)
    x1 = nc.dram_tensor("x1", [4096, TOK], F32, kind="ExternalInput").ap()
    wq1 = nc.dram_tensor("wq1", [32, 128, 4096], F32, kind="ExternalInput").ap()
    wk1 = nc.dram_tensor("wk1", [16, 128, 8192], F32, kind="ExternalInput").ap()
    wv1 = nc.dram_tensor("wv1", [16, 128, 8192], F32, kind="ExternalInput").ap()
    bqv = nc.dram_tensor("bqv", [128, 32], F32, kind="ExternalInput").ap()
    bkv = nc.dram_tensor("bkv", [1, 8192], F32, kind="ExternalInput").ap()
    bwt = nc.dram_tensor("bwt", [3, 16], F32, kind="ExternalInput").ap()
    biv = nc.dram_tensor("biv", [3, 64], F32, kind="ExternalInput").ap()
    revp = nc.dram_tensor("revp", [16, 64], F32, kind="ExternalInput").ap()
    diagp = nc.dram_tensor("diagp", [16, 64], F32, kind="ExternalInput").ap()
    wr = nc.dram_tensor("wr", [128, 2048], F32, kind="ExternalInput").ap()
    a1 = nc.dram_tensor("a1", [8, 128, 4096], F32, kind="ExternalInput").ap()
    b1 = nc.dram_tensor("b1", [32, 128, 1024], F32, kind="ExternalInput").ap()
    exp1 = nc.dram_tensor("exp1", [64, 1024], BF16, kind="ExternalInput").ap()
    qed = nc.dram_tensor("qed", [4096, TOK], BF16, kind="ExternalOutput").ap()
    kvd2 = nc.dram_tensor("kvd2", [TOK, 8192], BF16, kind="ExternalOutput").ap()
    g2d = nc.dram_tensor("g2d", [4096, TOK], BF16, kind="ExternalOutput").ap()
    mall = nc.dram_tensor("mall", [16, 64], F32, kind="Internal")
    bdall = nc.dram_tensor("bdall", [16, 128, 128], F32, kind="Internal")

    with contextlib.ExitStack() as stk:
        sb = lambda name, shape, dt: stk.enter_context(nc.sbuf_tensor(name, shape, dt))
        xn_s = sb("xn_s", [128, DCH * TOK], BF16)
        xb = [sb("xb0", [128, TOK], F32), sb("xb1", [128, TOK], F32)]
        xsq = sb("xsq", [128, TOK], F32)
        nm_b = sb("nm_b", [128, TOK], F32)
        rs_b = sb("rs_b", [128, TOK], F32)
        ones_f = sb("ones_f", [128, 1], F32)
        ones1_f = sb("ones1_f", [1, 128], F32)
        onesc_b = sb("onesc_b", [1, 128], BF16)
        w2 = [sb("w20", [128, 4096], BF16), sb("w21", [128, 4096], BF16)]
        kw = [sb("kw0", [128, 8192], BF16), sb("kw1", [128, 8192], BF16)]
        bwt_s = sb("bwt_s", [3, 16], F32)
        biv_s = sb("biv_s", [3, 64], F32)
        revp_s = sb("revp_s", [16, 64], F32)
        diagp_s = sb("diagp_s", [16, 64], F32)
        mfl_s = sb("mfl_s", [16, 64], F32)
        zt_s = sb("zt_s", [128, 128], F32)
        bd_s = [sb(f"bd_s{i}", [128, 128], BF16) for i in range(2)]
        bq_s = sb("bq_s", [128, 32], F32)
        bkv_s = [sb(f"bkv_s{i}", [1, 256], BF16) for i in range(2)]
        qtmp = [sb(f"qtmp{i}", [128, 512], BF16) for i in range(2)]
        qeo = [sb(f"qeo{i}", [128, 512], BF16) for i in range(2)]
        kvo = [sb(f"kvo{i}", [128, 256], BF16) for i in range(2)]
        wr_s = sb("wr_s", [128, 2048], BF16)
        a1s = [sb(f"a1s{i}", [128, 4096], BF16) for i in range(2)]
        b1s = [sb(f"b1s{i}", [128, 1024], BF16) for i in range(2)]
        exp1_s = sb("exp1_s", [64, 1024], BF16)
        lg_s = sb("lg_s", [128, 64], F32)
        ltmp = sb("ltmp", [128, 64], F32)
        e_s = sb("e_s", [128, 64], F32)
        m1_s = sb("m1_s", [128, 1], F32)
        nm1_s = sb("nm1_s", [128, 1], F32)
        th_s = sb("th_s", [128, 1], F32)
        mge_s = sb("mge_s", [128, 64], F32)
        ssum = sb("ssum", [128, 1], F32)
        rec_s = sb("rec_s", [128, 1], F32)
        gte = sb("gte", [128, 64], BF16)
        gT_s = sb("gT_s", [64, TOK], BF16)
        ge_s = [sb(f"ge_s{i}", [128, TOK], BF16) for i in range(2)]
        hw2_s = sb("hw2_s", [128, 8 * TOK], BF16)
        go = [sb(f"go{i}", [128, 512], BF16) for i in range(2)]
        ident = sb("ident", [128, 128], BF16)
        P = [stk.enter_context(nc.psum_tensor(f"P{i}", [128, 512], F32)) for i in range(7)]
        PT64 = stk.enter_context(nc.psum_tensor("PT64", [64, 128], BF16))
        s = _mk_sems(stk, nc, ["ld", "xl", "w", "mm", "ac", "dv", "st", "ph", "bd"])

        for ap_, src_ in ((bwt_s, bwt), (biv_s, biv), (revp_s, revp), (diagp_s, diagp),
                          (bq_s, bqv), (exp1_s, exp1)):
            s["ld"].inc(nc.sync.dma_start(out=ap_[:, :], in_=src_[:, :]), 16)
        s["ld"].inc(nc.gpsimd.dma_start(out=wr_s[:, :], in_=wr[:, :]), 16)
        nc.vector.memset(ones_f[:, :], 1.0)
        nc.vector.memset(ones1_f[:, :], 1.0)
        nc.vector.memset(onesc_b[:, :], 1.0)
        s["dv"].inc(nc.vector.memset(zt_s[:, :], 0.0), 1)
        nc.gpsimd.memset(ident[:, :], 0.0)
        s["dv"].inc(nc.gpsimd.affine_select(
            out=ident[:, :], in_=ident[:, :], compare_op=ALU.not_equal,
            fill=1.0, base=0, pattern=[[-1, 128]], channel_multiplier=1), 1)
        nc.tensor.wait_ge(s["dv"].h, s["dv"].n)
        nc.tensor.wait_ge(s["ld"].h, s["ld"].n)

        mmw = s["mm"].inc(nc.tensor.matmul(P[6][0:16, 0:64], bwt_s[:, :], biv_s[:, :],
                                           start=True, stop=True), 1)
        nc.vector.wait_ge(s["mm"].h, mmw)
        v1 = s["dv"].inc(nc.vector.tensor_mul(mfl_s[:, :], P[6][0:16, 0:64], revp_s[:, :]), 1)
        nc.vector.wait_ge(s["dv"].h, v1)
        v2 = s["dv"].inc(nc.vector.tensor_add(mfl_s[:, :], mfl_s[:, :], diagp_s[:, :]), 1)
        nc.sync.wait_ge(s["dv"].h, v2)
        s["bd"].inc(nc.sync.dma_start(out=mall.ap(), in_=mfl_s[:, :]), 16)
        for hh in range(16):
            s["bd"].inc(nc.sync.dma_start(out=bdall.ap()[hh, :, :], in_=zt_s[:, :]), 16)
        nc.sync.wait_ge(s["bd"].h, s["bd"].n)
        for bb_ in range(16):
            s["bd"].inc(nc.sync.dma_start(
                out=bdall.ap()[:, 8 * bb_:8 * (bb_ + 1), 8 * bb_:8 * (bb_ + 1)],
                in_=mall.ap().rearrange("h (i j) -> h i j", i=8)), 16)

        rows = [nm_b[0:1, :], xsq[0:1, :], xb[0][0:1, :], xb[1][0:1, :], rs_b[0:1, :]]
        emit_ln(nc, s, P, x1, xn_s, dict(
            xb=xb, xsq=xsq, rows=rows, nm_b=nm_b, rs_b=rs_b, ones_f=ones_f, ones1_f=ones1_f))
        _barrier(nc, s["ph"])

        mm_q = {}
        ac_q = {}
        st_q = {}
        nc.gpsimd.wait_ge(s["bd"].h, s["bd"].n)
        for oc in range(32):
            h = oc // 2
            wbuf = w2[oc % 2]
            if oc >= 2:
                nc.gpsimd.wait_ge(s["mm"].h, mm_q[(oc - 2, 1)])
            s["w"].inc(nc.gpsimd.dma_start(out=wbuf[:, :], in_=wq1[oc, :, :]), 16)
            if oc % 2 == 0:
                if h >= 2:
                    nc.gpsimd.wait_ge(s["mm"].h, mm_q[(oc - 2, 1)])
                s["w"].inc(nc.gpsimd.dma_start(out=bd_s[h % 2][:, :],
                                               in_=bdall.ap()[h, :, :]), 16)
            nc.tensor.wait_ge(s["w"].h, s["w"].n)
            for tb in range(2):
                psq = P[(oc % 2) * 2 + tb]
                psr = P[4 + tb]
                if oc >= 2:
                    nc.tensor.wait_ge(s["ac"].h, ac_q[(oc - 2, tb)])
                for dc in range(DCH):
                    inst = nc.tensor.matmul(
                        psq[:, :], wbuf[:, 128 * dc:128 * (dc + 1)],
                        xn_s[:, TOK * dc + 512 * tb:TOK * dc + 512 * (tb + 1)],
                        start=(dc == 0), stop=(dc == DCH - 1))
                mmp = s["mm"].inc(inst, 1)
                nc.vector.wait_ge(s["mm"].h, mmp)
                if oc >= 1:
                    nc.vector.wait_ge(s["mm"].h, mm_q[(oc - 1, tb)])
                dvb_ = s["dv"].inc(nc.vector.tensor_scalar_add(
                    qtmp[tb][:, :], psq[:, :], bq_s[:, oc:oc + 1]), 1)
                nc.tensor.wait_ge(s["dv"].h, dvb_)
                if oc >= 1:
                    nc.tensor.wait_ge(s["ac"].h, ac_q[(oc - 1, tb)])
                mm_q[(oc, tb)] = s["mm"].inc(nc.tensor.matmul(
                    psr[:, :], bd_s[h % 2][:, :], qtmp[tb][:, :], start=True, stop=True), 1)
                nc.scalar.wait_ge(s["mm"].h, mm_q[(oc, tb)])
                if oc >= 1:
                    nc.scalar.wait_ge(s["st"].h, st_q[(oc - 1, tb)])
                ac_q[(oc, tb)] = s["ac"].inc(nc.scalar.copy(qeo[tb][:, :], psr[:, :]), 1)
                nc.sync.wait_ge(s["ac"].h, ac_q[(oc, tb)])
                st_q[(oc, tb)] = s["st"].inc(nc.sync.dma_start(
                    out=qed[128 * oc:128 * (oc + 1), 512 * tb:512 * (tb + 1)],
                    in_=qeo[tb][:, :]), 16)
        _barrier(nc, s["ph"])

        mm_kv = {}
        ac_kv = {}
        st_kv = {}
        ld_b = {}
        for part in range(32):
            isv = part >= 16
            qq = part % 16
            wsl = kw[part % 2]
            if part >= 2:
                nc.gpsimd.wait_ge(s["mm"].h, mm_kv[(part - 2, 7)])
            s["w"].inc(nc.gpsimd.dma_start(out=wsl[:, :], in_=(wv1 if isv else wk1)[qq, :, :]), 16)
            bcol = 4096 * (1 if isv else 0) + 256 * qq
            if part >= 2:
                nc.sync.wait_ge(s["mm"].h, mm_kv[(part - 2, 0)])
            ld_b[part] = s["w"].inc(nc.gpsimd.dma_start(
                out=bkv_s[part % 2][:, :], in_=bkv[0:1, bcol:bcol + 256]), 16)
            nc.tensor.wait_ge(s["w"].h, s["w"].n)
            for tc in range(8):
                pb = P[tc % 2]
                idx = part * 8 + tc
                if idx >= 2:
                    pk, pt = divmod(idx - 2, 8)
                    nc.tensor.wait_ge(s["ac"].h, ac_kv[(pk, pt)])
                nc.tensor.matmul(pb[:, 0:256], onesc_b[:, :], bkv_s[part % 2][:, :],
                                 start=True, stop=False)
                for dc in range(DCH):
                    inst = nc.tensor.matmul(
                        pb[:, 0:256],
                        xn_s[:, TOK * dc + 128 * tc:TOK * dc + 128 * (tc + 1)],
                        wsl[:, 256 * dc:256 * (dc + 1)],
                        start=False, stop=(dc == DCH - 1))
                mm_kv[(part, tc)] = s["mm"].inc(inst, 1)
                kb = kvo[tc % 2]
                nc.scalar.wait_ge(s["mm"].h, mm_kv[(part, tc)])
                if idx >= 2:
                    nc.scalar.wait_ge(s["st"].h, st_kv[idx - 2])
                ac_kv[(part, tc)] = s["ac"].inc(nc.scalar.copy(kb[:, :], pb[:, 0:256]), 1)
                nc.sync.wait_ge(s["ac"].h, ac_kv[(part, tc)])
                st_kv[idx] = s["st"].inc(nc.sync.dma_start(
                    out=kvd2[128 * tc:128 * (tc + 1), bcol:bcol + 256], in_=kb[:, :]), 16)
        _barrier(nc, s["ph"])

        def dv(fn):
            v = s["dv"].inc(fn(), 1)
            nc.vector.wait_ge(s["dv"].h, v)
            return v
        for tc in range(8):
            for dc in range(DCH):
                inst = nc.tensor.matmul(
                    P[6][:, 0:64],
                    xn_s[:, TOK * dc + 128 * tc:TOK * dc + 128 * (tc + 1)],
                    wr_s[:, 64 * dc:64 * (dc + 1)],
                    start=(dc == 0), stop=(dc == DCH - 1))
            mmv = s["mm"].inc(inst, 1)
            nc.vector.wait_ge(s["mm"].h, mmv)
            dv(lambda: nc.vector.tensor_copy(lg_s[:, :], P[6][:, 0:64]))
            dv(lambda: nc.vector.tensor_copy(ltmp[:, :], lg_s[:, :]))
            dv(lambda: nc.vector.reduce_max(m1_s[:, :], lg_s[:, :], axis=AX.X))
            for it in range(15):
                mref = m1_s if it == 0 else th_s
                dv(lambda: nc.vector.tensor_scalar(
                    mge_s[:, :], ltmp[:, :], mref[:, 0:1], -1e30,
                    op0=ALU.is_equal, op1=ALU.mult))
                dv(lambda: nc.vector.tensor_add(ltmp[:, :], ltmp[:, :], mge_s[:, :]))
                dv(lambda: nc.vector.reduce_max(th_s[:, :], ltmp[:, :], axis=AX.X))
            dv(lambda: nc.vector.tensor_scalar(nm1_s[:, :], m1_s[:, :], -1.0, None, ALU.mult))
            nc.scalar.wait_ge(s["dv"].h, s["dv"].n)
            acv = s["ac"].inc(nc.scalar.activation(e_s[:, :], lg_s[:, :], AF.Exp,
                                                   bias=nm1_s[:, 0:1]), 1)
            nc.vector.wait_ge(s["ac"].h, acv)
            dv(lambda: nc.vector.tensor_scalar(mge_s[:, :], lg_s[:, :], th_s[:, 0:1], None,
                                               op0=ALU.is_ge))
            dv(lambda: nc.vector.tensor_mul(e_s[:, :], e_s[:, :], mge_s[:, :]))
            dv(lambda: nc.vector.reduce_sum(ssum[:, :], e_s[:, :], axis=AX.X))
            dv(lambda: nc.vector.reciprocal(rec_s[:, :], ssum[:, :]))
            dvv = dv(lambda: nc.vector.tensor_scalar(gte[:, :], e_s[:, :], rec_s[:, 0:1], None,
                                                     op0=ALU.mult))
            nc.tensor.wait_ge(s["dv"].h, dvv)
            if tc >= 1:
                nc.tensor.wait_ge(s["ac"].h, s["ac"].n)
            mmt = s["mm"].inc(nc.tensor.transpose(PT64[0:64, :], gte[:, :], ident[:, :]), 1)
            nc.scalar.wait_ge(s["mm"].h, mmt)
            s["ac"].inc(nc.scalar.copy(gT_s[0:64, 128 * tc:128 * (tc + 1)], PT64[0:64, :]), 1)
        nc.tensor.wait_ge(s["ac"].h, s["ac"].n)
        dv_hw = {}
        mm_rc = {}
        for rc in range(8):
            abuf = a1s[rc % 2]
            if rc >= 2:
                nc.gpsimd.wait_ge(s["mm"].h, mm_rc[(rc - 2, 1)])
            s["w"].inc(nc.gpsimd.dma_start(out=abuf[:, :], in_=a1[rc, :, :]), 16)
            for tb in range(2):
                pge = P[4 + tb]
                if rc >= 1:
                    nc.tensor.wait_ge(s["dv"].h, dv_hw[(rc - 1, tb)])
                mmv = s["mm"].inc(nc.tensor.matmul(
                    pge[:, :], exp1_s[:, 128 * rc:128 * (rc + 1)],
                    gT_s[0:64, 512 * tb:512 * (tb + 1)], start=True, stop=True), 1)
                nc.scalar.wait_ge(s["mm"].h, mmv)
                s["ac"].inc(nc.scalar.copy(ge_s[tb][:, 0:512], pge[:, :]), 1)
            acge = s["ac"].n
            nc.tensor.wait_ge(s["w"].h, s["w"].n)
            for tb in range(2):
                pb = P[tb]
                if rc >= 2:
                    nc.tensor.wait_ge(s["dv"].h, dv_hw[(rc - 2, tb)])
                for dc in range(DCH):
                    inst = nc.tensor.matmul(
                        pb[:, :], abuf[:, 128 * dc:128 * (dc + 1)],
                        xn_s[:, TOK * dc + 512 * tb:TOK * dc + 512 * (tb + 1)],
                        start=(dc == 0), stop=(dc == DCH - 1))
                mm_rc[(rc, tb)] = s["mm"].inc(inst, 1)
                nc.vector.wait_ge(s["mm"].h, mm_rc[(rc, tb)])
                nc.vector.wait_ge(s["ac"].h, acge)
                dv_hw[(rc, tb)] = s["dv"].inc(nc.vector.tensor_mul(
                    hw2_s[:, TOK * rc + 512 * tb:TOK * rc + 512 * (tb + 1)], pb[:, :],
                    ge_s[tb][:, 0:512]), 1)
        mm_g = {}
        ac_g = {}
        st_g = {}
        nc.tensor.wait_ge(s["dv"].h, s["dv"].n)
        nc.tensor.wait_ge(s["ac"].h, s["ac"].n)
        for oc in range(32):
            bsl = b1s[oc % 2]
            if oc >= 2:
                nc.gpsimd.wait_ge(s["mm"].h, mm_g[(oc - 2, 1)])
            s["w"].inc(nc.gpsimd.dma_start(out=bsl[:, :], in_=b1[oc, :, :]), 16)
            nc.tensor.wait_ge(s["w"].h, s["w"].n)
            for tb in range(2):
                pb = P[2 + (oc % 2) * 2 + tb]
                if oc >= 2:
                    nc.tensor.wait_ge(s["ac"].h, ac_g[(oc - 2, tb)])
                for rc in range(8):
                    inst = nc.tensor.matmul(
                        pb[:, :], bsl[:, 128 * rc:128 * (rc + 1)],
                        hw2_s[:, TOK * rc + 512 * tb:TOK * rc + 512 * (tb + 1)],
                        start=(rc == 0), stop=(rc == 7))
                mm_g[(oc, tb)] = s["mm"].inc(inst, 1)
                nc.scalar.wait_ge(s["mm"].h, mm_g[(oc, tb)])
                if oc >= 1:
                    nc.scalar.wait_ge(s["st"].h, st_g[(oc - 1, tb)])
                ac_g[(oc, tb)] = s["ac"].inc(nc.scalar.activation(go[tb][:, :], pb[:, :], AF.Silu), 1)
                nc.sync.wait_ge(s["ac"].h, ac_g[(oc, tb)])
                st_g[(oc, tb)] = s["st"].inc(nc.sync.dma_start(
                    out=g2d[128 * oc:128 * (oc + 1), 512 * tb:512 * (tb + 1)], in_=go[tb][:, :]), 16)
        nc.sync.wait_ge(s["st"].h, s["st"].n)
    return nc



def build_l1b():
    nc = bass.Bass("TRN2", target_bir_lowering=False)
    x1 = nc.dram_tensor("x1", [4096, TOK], F32, kind="ExternalInput").ap()
    qed = nc.dram_tensor("qed", [4096, TOK], BF16, kind="ExternalInput").ap()
    g2d = nc.dram_tensor("g2d", [4096, TOK], BF16, kind="ExternalInput").ap()
    kvg2 = nc.dram_tensor("kvg2", [4096, 8192], BF16, kind="ExternalInput").ap()
    outd = nc.dram_tensor("outd", [4096, TOK], F32, kind="ExternalOutput").ap()

    with contextlib.ExitStack() as stk:
        sb = lambda name, shape, dt: stk.enter_context(nc.sbuf_tensor(name, shape, dt))
        kfm = [sb(f"kfm{i}", [128, 4096], BF16) for i in range(2)]
        vkm = sb("vkm", [128, 8192], BF16)
        q_s = [sb(f"q_s{i}", [128, TOK], BF16) for i in range(2)]
        x1_s = [sb(f"x1_s{i}", [128, TOK], F32) for i in range(2)]
        g2_s = [sb(f"g2_s{i}", [128, TOK], F32) for i in range(2)]
        ebuf = [sb(f"ebuf{i}", [128, 512], BF16) for i in range(3)]
        rb_s = sb("rb_s", [128, 512], F32)
        rrow = sb("rrow", [1, 512], F32)
        s2n = sb("s2n", [128, 512], F32)
        t_s = sb("t_s", [128, 512], F32)
        oo = [sb(f"oo{i}", [128, 512], F32) for i in range(2)]
        biasB = sb("biasB", [128, 1], F32)
        ones_b = sb("ones_b", [128, 1], BF16)
        ones1_f = sb("ones1_f", [1, 128], F32)
        P = [stk.enter_context(nc.psum_tensor(f"P{i}", [128, 512], F32)) for i in range(7)]
        s = _mk_sems(stk, nc, ["kf", "mm", "ac", "dv", "st", "ph"])

        nc.vector.memset(biasB[:, :], -EXPB)
        nc.vector.memset(ones_b[:, :], 1.0)
        s["dv"].inc(nc.vector.memset(ones1_f[:, :], 1.0), 1)
        nc.tensor.wait_ge(s["dv"].h, s["dv"].n)

        mm_headdone = {}
        ac_exp = {}
        mm_ctx = {}
        ac_rb = {}
        dv_evac = {}
        st_out = {}
        for h in range(16):
            if h >= 1:
                nc.sync.wait_ge(s["mm"].h, mm_headdone[h - 1])
                nc.gpsimd.wait_ge(s["mm"].h, mm_headdone[h - 1])
                nc.gpsimd.wait_ge(s["dv"].h, dv_evac[(h - 1, 1, 1)])
                nc.sync.wait_ge(s["dv"].h, dv_evac[(h - 1, 1, 1)])
            s["kf"].inc(nc.sync.dma_start(out=kfm[0][:, :],
                                          in_=kvg2[:, 256 * h:256 * h + 128], transpose=True), 16)
            s["kf"].inc(nc.sync.dma_start(out=kfm[1][:, :],
                                          in_=kvg2[:, 256 * h + 128:256 * h + 256], transpose=True), 16)
            s["kf"].inc(nc.gpsimd.dma_start(
                out=vkm[:, :].rearrange("p (a f) -> p a f", a=32),
                in_=kvg2[:, 4096 + 256 * h:4096 + 256 * (h + 1)].rearrange(
                    "(a p) f -> p a f", p=128)), 16)
            s["kf"].inc(nc.sync.dma_start(out=q_s[0][:, :],
                                          in_=qed[256 * h:256 * h + 128, :]), 16)
            s["kf"].inc(nc.sync.dma_start(out=q_s[1][:, :],
                                          in_=qed[256 * h + 128:256 * h + 256, :]), 16)
            s["kf"].inc(nc.sync.dma_start(out=x1_s[0][:, :],
                                          in_=x1[256 * h:256 * h + 128, :]), 16)
            s["kf"].inc(nc.sync.dma_start(out=x1_s[1][:, :],
                                          in_=x1[256 * h + 128:256 * h + 256, :]), 16)
            s["kf"].inc(nc.gpsimd.dma_start(out=g2_s[0][:, :],
                                            in_=g2d[256 * h:256 * h + 128, :]), 16)
            s["kf"].inc(nc.gpsimd.dma_start(out=g2_s[1][:, :],
                                            in_=g2d[256 * h + 128:256 * h + 256, :]), 16)
            nc.tensor.wait_ge(s["kf"].h, s["kf"].n)
            for qb in range(2):
                for i in range(32):
                    if i >= 2:
                        nc.tensor.wait_ge(s["ac"].h, ac_exp[(h, qb, i - 2)])
                    nc.tensor.matmul(P[i % 2][:, :], kfm[0][:, 128 * i:128 * (i + 1)],
                                     q_s[0][:, 512 * qb:512 * (qb + 1)], start=True, stop=False)
                    mmv = s["mm"].inc(nc.tensor.matmul(
                        P[i % 2][:, :], kfm[1][:, 128 * i:128 * (i + 1)],
                        q_s[1][:, 512 * qb:512 * (qb + 1)], start=False, stop=True), 1)
                    nc.scalar.wait_ge(s["mm"].h, mmv)
                    if i >= 3:
                        nc.scalar.wait_ge(s["mm"].h, mm_ctx[(h, qb, i - 3)])
                    ac_exp[(h, qb, i)] = s["ac"].inc(nc.scalar.activation(
                        ebuf[i % 3][:, :], P[i % 2][:, :], AF.Exp, bias=biasB[:, 0:1]), 1)
                    nc.tensor.wait_ge(s["ac"].h, ac_exp[(h, qb, i)])
                    if i == 0 and (h, qb) != (0, 0):
                        ph, pqb = (h, qb - 1) if qb == 1 else (h - 1, 1)
                        nc.tensor.wait_ge(s["dv"].h, dv_evac[(ph, pqb, 1)])
                    nc.tensor.matmul(P[4][:, :], vkm[:, 256 * i:256 * i + 128],
                                     ebuf[i % 3][:, :], start=(i == 0), stop=(i == 31))
                    nc.tensor.matmul(P[5][:, :], vkm[:, 256 * i + 128:256 * i + 256],
                                     ebuf[i % 3][:, :], start=(i == 0), stop=(i == 31))
                    mm_ctx[(h, qb, i)] = s["mm"].inc(nc.tensor.matmul(
                        P[6][0:1, :], ones_b[:, :], ebuf[i % 3][:, :],
                        start=(i == 0), stop=(i == 31)), 1)
                nc.vector.wait_ge(s["mm"].h, mm_ctx[(h, qb, 31)])
                dvr = s["dv"].inc(nc.vector.reciprocal(rrow[0:1, :], P[6][0:1, :]), 1)
                nc.tensor.wait_ge(s["dv"].h, dvr)
                if (h, qb) != (0, 0):
                    ph, pqb = (h, qb - 1) if qb == 1 else (h - 1, 1)
                    nc.tensor.wait_ge(s["ac"].h, ac_rb[(ph, pqb)])
                mmb = s["mm"].inc(nc.tensor.matmul(P[2][:, :], ones1_f[:, :], rrow[0:1, :],
                                                   start=True, stop=True), 1)
                nc.scalar.wait_ge(s["mm"].h, mmb)
                ac_rb[(h, qb)] = s["ac"].inc(nc.scalar.copy(rb_s[:, :], P[2][:, :]), 1)
                nc.vector.wait_ge(s["ac"].h, ac_rb[(h, qb)])
                for cc in range(2):
                    vv = s["dv"].inc(nc.vector.tensor_mul(s2n[:, :], P[4 + cc][:, :], rb_s[:, :]), 1)
                    nc.vector.wait_ge(s["dv"].h, vv)
                    vv = s["dv"].inc(nc.vector.tensor_mul(
                        t_s[:, :], s2n[:, :], g2_s[cc][:, 512 * qb:512 * (qb + 1)]), 1)
                    nc.vector.wait_ge(s["dv"].h, vv)
                    if (h, qb, cc) != (0, 0, 0):
                        pidx = (h, qb, cc - 1) if cc == 1 else (
                            (h, qb - 1, 1) if qb == 1 else (h - 1, 1, 1))
                        nc.vector.wait_ge(s["st"].h, st_out[pidx])
                    dv_evac[(h, qb, cc)] = s["dv"].inc(nc.vector.tensor_add(
                        oo[cc][:, :], t_s[:, :], x1_s[cc][:, 512 * qb:512 * (qb + 1)]), 1)
                    nc.sync.wait_ge(s["dv"].h, dv_evac[(h, qb, cc)])
                    st_out[(h, qb, cc)] = s["st"].inc(nc.sync.dma_start(
                        out=outd[256 * h + 128 * cc:256 * h + 128 * (cc + 1),
                                 512 * qb:512 * (qb + 1)],
                        in_=oo[cc][:, :]), 16)
            mm_headdone[h] = mm_ctx[(h, 1, 31)]
        nc.sync.wait_ge(s["st"].h, s["st"].n)
    return nc



_NC_CACHE = {}


def _get_nc(name, builder):
    if name not in _NC_CACHE:
        _NC_CACHE[name] = builder()
    return _NC_CACHE[name]


def _prep_consts():
    cos, sin = _rope_cos_sin()
    rev, biv = _clifford_rev_biv()
    consts = {}
    for g in range(4):
        own = _own_tokens(g)
        cq = (cos[own, :].T * S128).astype(np.float32)
        sq = (sin[own, :].T * S128).astype(np.float32)
        consts[f"cosq{g}"] = cq.astype(ml_dtypes.bfloat16)
        consts[f"sinq{g}"] = sq
        ck = cos[own, :].astype(np.float32)
        sk = sin[own, :].astype(np.float32)
        skS = sk.copy()
        skS[:, :64] *= -1.0
        consts[f"cosk{g}"] = np.ascontiguousarray(ck.reshape(8, 128, 128).transpose(1, 0, 2).reshape(128, 1024))
        consts[f"sink{g}"] = np.ascontiguousarray(skS.reshape(8, 128, 128).transpose(1, 0, 2).reshape(128, 1024))
        gathered = np.concatenate([_own_tokens(r) for r in range(4)])
        qabs = own
        m = np.zeros((2, 32, 128, 512), dtype=ml_dtypes.bfloat16)
        for qb in range(2):
            qpos = qabs[512 * qb:512 * (qb + 1)]
            for kt in range(32):
                kpos = gathered[128 * kt:128 * (kt + 1)]
                m[qb, kt] = (kpos[:, None] <= qpos[None, :]).astype(ml_dtypes.bfloat16)
        consts[f"mask{g}"] = m
    prot = np.zeros((128, 128), dtype=np.float32)
    for j in range(64):
        prot[j + 64, j] = -1.0
    for j in range(64, 128):
        prot[j - 64, j] = 1.0
    consts["prot"] = prot.astype(ml_dtypes.bfloat16)
    consts["exp0"] = np.kron(np.eye(8, dtype=np.float32),
                             np.ones((1, 16), np.float32)).astype(ml_dtypes.bfloat16)
    consts["exp1"] = (2.0 * np.kron(np.eye(64, dtype=np.float32),
                                    np.ones((1, 16), np.float32))).astype(ml_dtypes.bfloat16)
    consts["biv"] = np.ascontiguousarray(biv.transpose(2, 0, 1).reshape(3, 64)).astype(np.float32)
    revrow = np.repeat(rev, 8)[None, :]
    consts["revp"] = np.broadcast_to(revrow, (16, 64)).astype(np.float32).copy()
    diag = np.zeros((8, 8), np.float32)
    for i in range(8):
        diag[i, i] = rev[i] * (1.0 / 16.0)
    consts["diagp"] = np.broadcast_to(diag.reshape(1, 64), (16, 64)).astype(np.float32).copy()
    return consts


def _prep_weights(inp):
    w = {}
    w["wq"] = _wimg(np.ascontiguousarray(inp["L0_attn_Wq"].T), 128)
    w["wkq"] = _wimg(np.ascontiguousarray(inp["L0_attn_Wk"].T), 256)
    w["wvq"] = _wimg(np.ascontiguousarray(inp["L0_attn_Wv"].T), 256)
    w["wo"] = _wimg(np.ascontiguousarray(inp["L0_attn_Wo"].T), 128)
    w["wb"] = _wimg(np.ascontiguousarray(inp["L0_ffn_W_base"].T), 128)
    w["wg"] = _img_cols(np.ascontiguousarray(inp["L0_ffn_W_gate"].T))
    w["a0"] = _img_cols(np.ascontiguousarray(inp["L0_ffn_A"].transpose(2, 0, 1).reshape(4096, 128)))
    w["b0"] = np.ascontiguousarray(inp["L0_ffn_B"].transpose(0, 2, 1).reshape(128, 4096))
    w["bb"] = np.ascontiguousarray(inp["L0_ffn_b_base"].reshape(32, 128).T)
    w["wq1"] = _wimg(np.ascontiguousarray(inp["L1_attn_Wq"].T), 128)
    w["wk1"] = _wimg(np.ascontiguousarray(inp["L1_attn_Wk"].T), 256)
    w["wv1"] = _wimg(np.ascontiguousarray(inp["L1_attn_Wv"].T), 256)
    w["bqv"] = np.ascontiguousarray(inp["L1_attn_bq"].reshape(32, 128).T)
    w["bkv"] = np.ascontiguousarray(
        np.concatenate([inp["L1_attn_bk"], inp["L1_attn_bv"]])[None, :])
    w["bwt"] = np.ascontiguousarray(inp["L1_attn_bias_weight"].T)
    w["wr"] = _img_cols(np.ascontiguousarray(inp["L1_ffn_Wr"].T))
    w["a1"] = _wimg(np.ascontiguousarray(inp["L1_ffn_A"].transpose(2, 0, 1).reshape(4096, 1024)), 128)
    b1cat = inp["L1_ffn_B"].transpose(0, 2, 1).reshape(1024, 4096)
    w["b1"] = np.ascontiguousarray(
        b1cat.reshape(8, 128, 32, 128).transpose(2, 1, 0, 3).reshape(32, 128, 1024))
    return w


def kernel(**inputs):
    consts = _prep_consts()
    w = _prep_weights(inputs)
    x = inputs["L0_attn_x"]
    cores = [(c // 4, c % 4) for c in range(8)]
    xT = [np.ascontiguousarray(x[b].T[:, _own_tokens(g)]) for b, g in cores]

    ncA = _get_nc("A", build_l0a)
    mapsA = []
    for c, (b, g) in enumerate(cores):
        mapsA.append(dict(
            xT=xT[c], wq=w["wq"], wkq=w["wkq"], wvq=w["wvq"], wb=w["wb"], wg=w["wg"],
            a0=w["a0"], b0=w["b0"], bb=w["bb"],
            cosq=consts[f"cosq{g}"], sinq=consts[f"sinq{g}"],
            cosk=consts[f"cosk{g}"], sink=consts[f"sink{g}"],
            prot=consts["prot"], exp0=consts["exp0"]))
    resA = run_bass_kernel_spmd(ncA, mapsA, core_ids=list(range(8))).results

    kvg = [np.concatenate([resA[4 * b + r]["kvd"] for r in range(4)], axis=0)
           for b in range(2)]

    ncB = _get_nc("B", build_l0b)
    mapsB = []
    for c, (b, g) in enumerate(cores):
        mapsB.append(dict(
            xT=xT[c], qd=resA[c]["qd"], gd=resA[c]["gd"], kvg=kvg[b],
            wo=w["wo"], mask=consts[f"mask{g}"]))
    resB = run_bass_kernel_spmd(ncB, mapsB, core_ids=list(range(8))).results

    ncC = _get_nc("C", build_l1a)
    mapsC = []
    for c, (b, g) in enumerate(cores):
        mapsC.append(dict(
            x1=resB[c]["x1d"], wq1=w["wq1"], wk1=w["wk1"], wv1=w["wv1"],
            bqv=w["bqv"], bkv=w["bkv"], bwt=w["bwt"], biv=consts["biv"],
            revp=consts["revp"], diagp=consts["diagp"], wr=w["wr"], a1=w["a1"],
            b1=w["b1"], exp1=consts["exp1"]))
    resC = run_bass_kernel_spmd(ncC, mapsC, core_ids=list(range(8))).results

    kvg2 = [np.concatenate([resC[4 * b + r]["kvd2"] for r in range(4)], axis=0)
            for b in range(2)]

    ncD = _get_nc("D", build_l1b)
    mapsD = []
    for c, (b, g) in enumerate(cores):
        mapsD.append(dict(
            x1=resB[c]["x1d"], qed=resC[c]["qed"], g2d=resC[c]["g2d"], kvg2=kvg2[b]))
    resD = run_bass_kernel_spmd(ncD, mapsD, core_ids=list(range(8))).results

    out = np.empty((B, T, E), dtype=np.float32)
    for c, (b, g) in enumerate(cores):
        out[b, _own_tokens(g), :] = resD[c]["outd"].T
    return out
